# version 04
# baseline (speedup 1.0000x reference)
import numpy as np
import concourse.bass as bass
import concourse.mybir as mybir
from concourse.bass_utils import run_bass_kernel_spmd
from concourse.alu_op_type import AluOpType
from contextlib import ExitStack

F32 = mybir.dt.float32
BF16 = mybir.dt.bfloat16
AF = mybir.ActivationFunctionType
AX = mybir.AxisListType.X

D = 1024
SEQ = 2048
NT = 16
DI = 8
PAIRS = 8
QBn = 4
KC = 16
SBn = 8
TTS = 2
FC = 32
QS = 0.35355339059327373


def build_nc(debug=False):
    nc = bass.Bass("TRN2", target_bir_lowering=False, detect_race_conditions=False)
    dbg = {}
    if debug:
        for nm, shp in [("d_gfrep", [128, 1024]), ("d_stats", [128, 1024]),
                        ("d_xnT", [128, 2048]), ("d_qT", [128, 2048]), ("d_kT", [128, 2048]),
                        ("d_vtok", [128, 2176]), ("d_ncs", [128, 8]), ("d_linv", [128, 2048]),
                        ("d_P", [128, 2048]), ("d_x1", [128, 2048]), ("d_ln2T", [128, 2048]),
                        ("d_gT", [128, 512]), ("d_sgate", [128, 8]), ("d_rt", [128, 48])]:
            dbg[nm] = nc.dram_tensor(nm, shp, F32, kind="ExternalOutput")

    x_in = nc.dram_tensor("x_in", [SEQ, D], F32, kind="ExternalInput")
    wqkv = nc.dram_tensor("wqkv", [8, D, 384], F32, kind="ExternalInput")
    wout_d = nc.dram_tensor("wout", [D, D], F32, kind="ExternalInput")
    w1_d = nc.dram_tensor("w1", [32, D, 128], F32, kind="ExternalInput")
    w2_d = nc.dram_tensor("w2", [4 * D, D], F32, kind="ExternalInput")
    gam_d = nc.dram_tensor("gam", [D], F32, kind="ExternalInput")
    b1_d = nc.dram_tensor("b1l", [128, 128], F32, kind="ExternalInput")
    b2_d = nc.dram_tensor("b2rep", [128, D], F32, kind="ExternalInput")
    rw_d = nc.dram_tensor("rwl", [128, 64], F32, kind="ExternalInput")
    id_d = nc.dram_tensor("ident", [128, 128], F32, kind="ExternalInput")
    ascr = nc.dram_tensor("attn_scratch", [D, SEQ], F32)
    out_d = nc.dram_tensor("out_x", [SEQ, D], F32, kind="ExternalOutput")

    with ExitStack() as st:
        def sb(name, shape, dt=F32):
            return st.enter_context(nc.sbuf_tensor(name, shape, dt))

        A1 = sb("arena1", [128, 10240])
        A2 = sb("arena2", [128, 16384])
        vtok = sb("vtok", [128, 16 * 136])
        attnT = sb("attnT", [128, 1024])
        wsl = sb("wsl", [128, 6144])
        lrep = sb("lrep", [128, 1024])
        nrm = sb("nrm", [128, 1024])
        aT = sb("aTb", [128, 2048])
        w1s = sb("w1s", [128, 2048])
        w2s = sb("w2s", [128, 2048])
        zb = sb("zb", [128, 1024])
        x2b = sb("x2b", [128, 1024])
        gfrep = sb("gfrep", [128, 1024])
        b2rep = sb("b2r", [128, 1024])
        ident = sb("idents", [128, 128])
        ones = sb("onesf", [128, 128])
        b1s = sb("b1s", [128, 128])
        rws = sb("rws", [128, 64])
        stats = sb("stats", [128, 1024])
        st2 = sb("st2", [128, 128])
        ncs = sb("ncs", [128, 8])
        cmax = sb("cmaxs", [128, 64])
        mqk = sb("mqks", [128, 16])
        nmax = sb("nmaxs", [128, 8])
        probs = sb("probss", [128, 8])
        lts = sb("lts", [128, 8])
        msk = sb("msks", [128, 8])
        rts = sb("rts", [128, 32])
        s_sb = sb("s_sbs", [128, 8])
        sqbf = sb("sqbf", [128, 2048], BF16)
        onesbf = sb("onesbf", [128, 128], BF16)

        pb = [st.enter_context(nc.psum_tensor("pb%d" % i, [128, 512], F32)) for i in range(8)]

        SEMS = ["sGam", "sB1", "sB2r", "sRw", "sId", "sMs", "sGf", "sGfe",
                "sXA0", "sXA1", "sSq1", "sSm1", "sVa1", "sSc1",
                "sR1s", "sXB0", "sXB1", "sN1", "sTr1", "sTe1", "sWd0", "sWd1", "sQg", "sQe",
                "sVt", "sVe", "sSqk", "sCb", "sCr", "sC", "sS", "sE", "sAv", "sLl",
                "sRp", "sRe", "sAT", "sAH0", "sAH1", "sWoD", "sAg0", "sAg1",
                "sX30", "sX31", "sW10", "sW11", "sW20", "sW21", "sY",
                "sRd", "sSq3", "sVa3", "sSc3", "sN3", "sTr3", "sTe3", "sRt", "sRm",
                "sRe3", "sG", "sF1", "sGe3", "sF2c", "sX2", "sOut", "sVr", "sSr"]
        S = {n: st.enter_context(nc.semaphore(n)) for n in SEMS}

        P_sb = lambda kc: A1[:, (kc % 4) * 512:(kc % 4) * 512 + 512]
        qT = lambda lo, n: A1[:, 2048 + lo:2048 + lo + n]
        kT = lambda lo, n: A1[:, 4096 + lo:4096 + lo + n]
        vT = lambda lo, n: A1[:, 6144 + lo:6144 + lo + n]

        _vr = [0]
        def vsy(instr, v):
            _vr[0] += 1
            instr.then_inc(S["sVr"], 1)
            v.wait_ge(S["sVr"], _vr[0])
        _sr = [0]
        def ssy(instr, sc):
            _sr[0] += 1
            instr.then_inc(S["sSr"], 1)
            sc.wait_ge(S["sSr"], _sr[0])

        with nc.Block() as blk:
            @blk.sync
            def _(sync):
                sync.dma_start(out=zb[0:1, 0:1024], in_=bass.AP(gam_d, 0, [[1024, 1], [1, 1024]])).then_inc(S["sGam"], 16)
                sync.dma_start(out=b1s[:], in_=bass.AP(b1_d, 0, [[128, 128], [1, 128]])).then_inc(S["sB1"], 16)
                sync.dma_start(out=b2rep[:], in_=bass.AP(b2_d, 0, [[1024, 128], [1, 1024]])).then_inc(S["sB2r"], 16)
                sync.dma_start(out=rws[:], in_=bass.AP(rw_d, 0, [[64, 128], [1, 64]])).then_inc(S["sRw"], 16)
                sync.dma_start(out=ident[:], in_=bass.AP(id_d, 0, [[128, 128], [1, 128]])).then_inc(S["sId"], 16)
                src = bass.AP(wqkv, 0 * D * 384, [[384, 128], [128 * 384, 8], [1, 384]])
                dst = bass.AP(wsl, 0 * 3072, [[6144, 128], [384, 8], [1, 384]])
                sync.dma_start(out=dst, in_=src).then_inc(S["sWd0"], 16)
                for t in range(NT):
                    if t >= 2:
                        sync.wait_ge(S["sSq1"], t - 1)
                        sync.wait_ge(S["sSm1"], t - 1)
                    sync.dma_start(out=aT[:, (t % 2) * 1024:(t % 2) * 1024 + 1024],
                                   in_=bass.AP(x_in, t * 128 * 1024, [[1024, 128], [1, 1024]])).then_inc(S["sXA1" if t % 2 else "sXA0"], 16)
                for t in range(NT):
                    if t >= 2:
                        sync.wait_ge(S["sN1"], t - 1)
                    else:
                        sync.wait_ge(S["sSq1"], 15 + t)
                        sync.wait_ge(S["sSm1"], 15 + t)
                    sync.dma_start(out=aT[:, (t % 2) * 1024:(t % 2) * 1024 + 1024],
                                   in_=bass.AP(x_in, t * 128 * 1024, [[1024, 128], [1, 1024]])).then_inc(S["sXB1" if t % 2 else "sXB0"], 16)

            @blk.gpsimd
            def _(g):
                g.memset(ones[:], 1.0)
                g.memset(stats[:], 1.0)
                g.memset(st2[:], 1.0)
                g.memset(bass.AP(vtok, 64, [[16 * 136, 128], [136, 16]]), 1.0)
                g.memset(bass.AP(vtok, 132, [[16 * 136, 128], [136, 16]]), 1.0).then_inc(S["sMs"], 1)

            @blk.tensor
            def _(tensor):
                tensor.wait_ge(S["sGam"], 16)
                tensor.wait_ge(S["sMs"], 1)
                tensor.matmul(pb[0][:, 0:512], ones[0:1, 0:128], zb[0:1, 0:512], start=True, stop=True)
                tensor.matmul(pb[1][:, 0:512], ones[0:1, 0:128], zb[0:1, 512:1024], start=True, stop=True).then_inc(S["sGf"], 1)
                tensor.wait_ge(S["sId"], 16)
                for t in range(NT):
                    tensor.wait_ge(S["sN1"], t + 1)
                    if t >= 1:
                        tensor.wait_ge(S["sTe1"], 2 * (t - 1) + 2)
                    for di in range(DI):
                        bank = 6 + di // 4
                        i = tensor.transpose(pb[bank][:, (di % 4) * 128:(di % 4) * 128 + 128],
                                             nrm[:, di * 128:di * 128 + 128], ident[:])
                        if di == 3 or di == 7:
                            i.then_inc(S["sTr1"], 1)

            @blk.scalar
            def _(sc):
                sc.wait_ge(S["sMs"], 1)
                sc.wait_ge(S["sGf"], 1)
                sc.activation(out=gfrep[:, 0:512], in_=pb[0][:, 0:512], func=AF.Copy, bias=1.0, scale=1.0)
                sc.activation(out=gfrep[:, 512:1024], in_=pb[1][:, 0:512], func=AF.Copy, bias=1.0, scale=1.0).then_inc(S["sGfe"], 1)
                sc.wait_ge(S["sMs"], 1)
                sc.activation(out=onesbf[:], in_=ones[:], func=AF.Copy, bias=0.0, scale=1.0)
                for t in range(NT):
                    sc.wait_ge(S["sXA1" if t % 2 else "sXA0"], 16 * (t // 2 + 1))
                    if t >= 1:
                        sc.wait_ge(S["sSq1"], t)
                    sc.activation(out=zb[:], in_=aT[:, (t % 2) * 1024:(t % 2) * 1024 + 1024],
                                  func=AF.Square, bias=0.0, scale=1.0,
                                  accum_out=stats[:, 64 + 4 * t:64 + 4 * t + 1]).then_inc(S["sSq1"], 1)
                sc.wait_ge(S["sVa1"], 1)
                ssy(sc.activation(out=stats[:, 576:640], in_=stats[:, 384:448], func=AF.Ln, bias=0.0, scale=1.0), sc)
                ssy(sc.activation(out=stats[:, 640:704], in_=stats[:, 576:640], func=AF.Exp, bias=0.0, scale=-0.5), sc)
                ssy(sc.activation(out=stats[:, 576:640], in_=stats[:, 512:576], func=AF.Ln, bias=0.0, scale=1.0), sc)
                sc.activation(out=stats[:, 704:768], in_=stats[:, 576:640], func=AF.Exp, bias=0.0, scale=0.5).then_inc(S["sSc1"], 1)
                for t in range(NT):
                    for half in range(2):
                        sc.wait_ge(S["sTr1"], 2 * t + half + 1)
                        outap = bass.AP(A2, half * 4 * 2048 + t * 128, [[16384, 128], [2048, 4], [1, 128]])
                        inap = bass.AP(pb[6 + half], 0, [[512, 128], [128, 4], [1, 128]])
                        sc.activation(out=outap, in_=inap, func=AF.Copy, bias=0.0, scale=1.0).then_inc(S["sTe1"], 1)

            @blk.vector
            def _(v):
                v.wait_ge(S["sMs"], 1)
                for t in range(NT):
                    v.wait_ge(S["sXA1" if t % 2 else "sXA0"], 16 * (t // 2 + 1))
                    v.tensor_reduce(out=stats[:, 4 * t:4 * t + 1], in_=aT[:, (t % 2) * 1024:(t % 2) * 1024 + 1024],
                                    axis=AX, op=AluOpType.add).then_inc(S["sSm1"], 1)
                v.wait_ge(S["sSq1"], 16)
                v.wait_ge(S["sSm1"], 16)
                vsy(v.tensor_scalar(out=stats[:, 128:192], in0=stats[:, 0:64], scalar1=1.0 / D, scalar2=None, op0=AluOpType.mult), v)
                vsy(v.tensor_scalar(out=stats[:, 192:256], in0=stats[:, 64:128], scalar1=1.0 / D, scalar2=None, op0=AluOpType.mult), v)
                vsy(v.scalar_tensor_tensor(out=stats[:, 256:320], in0=stats[:, 128:192], scalar=-1.0,
                                           in1=stats[:, 128:192], op0=AluOpType.mult, op1=AluOpType.mult), v)
                vsy(v.tensor_tensor(out=stats[:, 320:384], in0=stats[:, 192:256], in1=stats[:, 256:320], op=AluOpType.add), v)
                vsy(v.tensor_scalar(out=stats[:, 384:448], in0=stats[:, 320:384], scalar1=1e-5, scalar2=None, op0=AluOpType.add), v)
                vsy(v.reciprocal(out=stats[:, 448:512], in_=stats[:, 384:448]), v)
                v.scalar_tensor_tensor(out=stats[:, 512:576], in0=stats[:, 320:384], scalar=float(D),
                                       in1=stats[:, 448:512], op0=AluOpType.mult, op1=AluOpType.mult).then_inc(S["sVa1"], 1)
                v.wait_ge(S["sSc1"], 1)
                vsy(v.tensor_scalar(out=stats[:, 768:832], in0=stats[:, 704:768], scalar1=1e-12, scalar2=None, op0=AluOpType.max), v)
                vsy(v.reciprocal(out=stats[:, 832:896], in_=stats[:, 768:832]), v)
                v.scalar_tensor_tensor(out=stats[:, 896:960], in0=stats[:, 640:704], scalar=32.0,
                                       in1=stats[:, 832:896], op0=AluOpType.mult, op1=AluOpType.mult).then_inc(S["sR1s"], 1)
                v.wait_ge(S["sGfe"], 1)
                for t in range(NT):
                    v.wait_ge(S["sXB1" if t % 2 else "sXB0"], 16 * (t // 2 + 1))
                    if t >= 1:
                        v.wait_ge(S["sTr1"], 2 * t)
                    v.tensor_scalar(out=zb[:], in0=aT[:, (t % 2) * 1024:(t % 2) * 1024 + 1024],
                                    scalar1=stats[:, 128 + 4 * t:128 + 4 * t + 1], scalar2=stats[:, 896 + 4 * t:896 + 4 * t + 1],
                                    op0=AluOpType.subtract, op1=AluOpType.mult)
                    v.tensor_tensor(out=nrm[:], in0=zb[:], in1=gfrep[:], op=AluOpType.mult).then_inc(S["sN1"], 1)

        for p in range(PAIRS):
            with nc.Block() as blk:
                @blk.sync
                def _(sync, p=p):
                    if p + 1 < PAIRS:
                        sl = (p + 1) % 2
                        sem = S["sWd1"] if sl == 1 else S["sWd0"]
                        src = bass.AP(wqkv, (p + 1) * D * 384, [[384, 128], [128 * 384, 8], [1, 384]])
                        dst = bass.AP(wsl, sl * 3072, [[6144, 128], [384, 8], [1, 384]])
                        sync.dma_start(out=dst, in_=src).then_inc(sem, 16)
                    for h in range(2):
                        for qb in range(QBn):
                            g3 = p * 8 + h * 4 + qb
                            sync.wait_ge(S["sAT"], g3 + 1)
                            dst = bass.AP(ascr, (p * 128 + h * 64) * SEQ + qb * 512, [[SEQ, 64], [1, 512]])
                            sync.dma_start(out=dst, in_=attnT[0:64, (g3 % 2) * 512:(g3 % 2) * 512 + 512]).then_inc(S["sAH1" if g3 % 2 else "sAH0"], 16)
                    if debug and p == 0:
                        sync.wait_ge(S["sGfe"], 1)
                        sync.dma_start(out=bass.AP(dbg["d_gfrep"], 0, [[1024, 128], [1, 1024]]), in_=gfrep[:]).then_inc(S["sOut"], 16)
                        sync.wait_ge(S["sR1s"], 1)
                        sync.dma_start(out=bass.AP(dbg["d_stats"], 0, [[1024, 128], [1, 1024]]), in_=stats[:]).then_inc(S["sOut"], 16)
                        sync.dma_start(out=bass.AP(dbg["d_xnT"], 0, [[2048, 128], [1, 2048]]), in_=A2[:, 0:2048]).then_inc(S["sOut"], 16)
                        sync.wait_ge(S["sQe"], 12)
                        sync.dma_start(out=bass.AP(dbg["d_qT"], 0, [[2048, 128], [1, 2048]]), in_=A1[:, 2048:4096]).then_inc(S["sOut"], 16)
                        sync.dma_start(out=bass.AP(dbg["d_kT"], 0, [[2048, 128], [1, 2048]]), in_=A1[:, 4096:6144]).then_inc(S["sOut"], 16)
                        sync.wait_ge(S["sVe"], 8)
                        sync.dma_start(out=bass.AP(dbg["d_vtok"], 0, [[2176, 128], [1, 2176]]), in_=vtok[:]).then_inc(S["sOut"], 16)
                        sync.wait_ge(S["sC"], 2)
                        sync.dma_start(out=bass.AP(dbg["d_ncs"], 0, [[8, 128], [1, 8]]), in_=ncs[:]).then_inc(S["sOut"], 16)
                        sync.wait_ge(S["sE"], 16)
                        sync.dma_start(out=bass.AP(dbg["d_P"], 0, [[2048, 128], [1, 2048]]), in_=A1[:, 0:2048]).then_inc(S["sOut"], 16)
                        sync.wait_ge(S["sLl"], 1)
                        sync.dma_start(out=bass.AP(dbg["d_linv"], 0, [[2048, 128], [1, 2048]]), in_=A1[:, 8192:10240]).then_inc(S["sOut"], 16)
                    if p == PAIRS - 1:
                        sync.wait_ge(S["sQg"], 96)
                        src = bass.AP(wout_d, 0, [[1024, 128], [128 * 1024, 8], [1, 1024]])
                        dst = bass.AP(A2, 4096, [[16384, 128], [1024, 8], [1, 1024]])
                        sync.dma_start(out=dst, in_=src).then_inc(S["sWoD"], 16)

                @blk.tensor
                def _(tensor, p=p):
                    wsem = S["sWd1"] if p % 2 == 1 else S["sWd0"]
                    tensor.wait_ge(wsem, 16 * (p // 2 + 1))
                    base = (p % 2) * 3072
                    for stn in range(3):
                        for qb in range(QBn):
                            g = stn * 4 + qb
                            if 12 * p + g - 1 > 0:
                                tensor.wait_ge(S["sQe"], 12 * p + g - 1)
                            for di in range(DI):
                                i = tensor.matmul(pb[g % 2][:, 0:512],
                                                  wsl[:, base + di * 384 + stn * 128: base + di * 384 + stn * 128 + 128],
                                                  A2[:, di * 2048 + qb * 512: di * 2048 + qb * 512 + 512],
                                                  start=(di == 0), stop=(di == DI - 1))
                            i.then_inc(S["sQg"], 1)
                    tensor.wait_ge(S["sQe"], 12 * p + 12)
                    for r in range(4):
                        if r >= 2:
                            tensor.wait_ge(S["sVe"], 8 * p + 2 * r - 2)
                        for j in range(4):
                            tt = 4 * r + j
                            i = tensor.transpose(pb[6 + r % 2][:, j * 128:j * 128 + 128],
                                                 vT(tt * 128, 128), ident[:])
                        i.then_inc(S["sVt"], 1)
                    for src_i in range(2):
                        tensor.wait_ge(S["sSqk"], 2 * p + src_i + 1)
                        for h in range(2):
                            for blkq in range(4):
                                i4 = src_i * 8 + h * 4 + blkq
                                if 16 * p + i4 - 1 > 0:
                                    tensor.wait_ge(S["sCr"], 16 * p + i4 - 1)
                                tensor.matmul(pb[i4 % 2][:, 0:512], onesbf[h * 64:h * 64 + 64, 0:128],
                                              sqbf[h * 64:h * 64 + 64, blkq * 512:blkq * 512 + 512],
                                              start=True, stop=True).then_inc(S["sCb"], 1)
                    tensor.wait_ge(S["sVe"], 8 * p + 8)
                    for h in range(2):
                        if h == 1:
                            tensor.wait_ge(S["sRe"], 8 * p + 4)
                        for qb in range(QBn):
                            sbase = p * 128 + (h * 4 + qb) * 16
                            if h == 1:
                                tensor.wait_ge(S["sAT"], 8 * p + qb + 1)
                            LAG = 2
                            for step in range(KC + LAG):
                                if step < KC:
                                    kc = step
                                    sidx = sbase + kc
                                    if sidx - 1 > 0:
                                        tensor.wait_ge(S["sE"], sidx - 1)
                                    tensor.matmul(pb[2 + kc % 2][:, 0:512],
                                                  A1[h * 64:h * 64 + 64, 4096 + kc * 128:4096 + kc * 128 + 128],
                                                  A1[h * 64:h * 64 + 64, 2048 + qb * 512:2048 + qb * 512 + 512],
                                                  start=True, stop=True).then_inc(S["sS"], 1)
                                if step >= LAG:
                                    kc = step - LAG
                                    sidx = sbase + kc
                                    tensor.wait_ge(S["sE"], sidx + 1)
                                    tensor.matmul(pb[4 + qb][0:65, 0:512],
                                                  vtok[:, kc * 136 + 68 * h: kc * 136 + 68 * h + 65],
                                                  P_sb(kc), start=(kc == 0), stop=(kc == KC - 1)).then_inc(S["sAv"], 1)
                        tensor.wait_ge(S["sLl"], 2 * p + h + 1)
                        for qb in range(QBn):
                            if 8 * p + 4 * h + qb > 0:
                                tensor.wait_ge(S["sRe"], 8 * p + 4 * h + qb)
                            tensor.matmul(pb[2][0:64, 0:512], ones[64:65, 0:64],
                                          A1[64:65, 8192 + qb * 512: 8192 + qb * 512 + 512],
                                          start=True, stop=True).then_inc(S["sRp"], 1)

                @blk.scalar
                def _(sc, p=p):
                    for stn in range(3):
                        for qb in range(QBn):
                            g = stn * 4 + qb
                            sc.wait_ge(S["sQg"], 12 * p + g + 1)
                            scale = QS if stn < 2 else 1.0
                            off = [2048, 4096, 6144][stn] + qb * 512
                            sc.activation(out=A1[:, off:off + 512], in_=pb[g % 2][:, 0:512],
                                          func=AF.Copy, bias=0.0, scale=scale).then_inc(S["sQe"], 1)
                    for r in range(4):
                        sc.wait_ge(S["sVt"], 4 * p + r + 1)
                        for part in range(2):
                            outap = bass.AP(vtok, (4 * r) * 136 + 68 * part, [[16 * 136, 128], [136, 4], [1, 64]])
                            inap = bass.AP(pb[6 + r % 2], part * 64, [[512, 128], [128, 4], [1, 64]])
                            sc.activation(out=outap, in_=inap, func=AF.Copy, bias=0.0, scale=1.0).then_inc(S["sVe"], 1)
                    sc.wait_ge(S["sQe"], 12 * p + 4)
                    sc.activation(out=sqbf[:], in_=A1[:, 2048:4096], func=AF.Square, bias=0.0, scale=1.0).then_inc(S["sSqk"], 1)
                    sc.wait_ge(S["sCb"], 16 * p + 8)
                    sc.activation(out=sqbf[:], in_=A1[:, 4096:6144], func=AF.Square, bias=0.0, scale=1.0).then_inc(S["sSqk"], 1)
                    for h in range(2):
                        for qb in range(QBn):
                            sbase = p * 128 + (h * 4 + qb) * 16
                            for kc in range(KC):
                                sidx = sbase + kc
                                sc.wait_ge(S["sS"], sidx + 1)
                                if kc == 0 and qb == 0:
                                    sc.wait_ge(S["sC"], 2 * p + h + 1)
                                if sidx - 3 > 0:
                                    sc.wait_ge(S["sAv"], sidx - 3)
                                sc.activation(out=P_sb(kc), in_=pb[2 + kc % 2][:, 0:512], func=AF.Exp,
                                              bias=ncs[:, 4 * h:4 * h + 1], scale=1.0).then_inc(S["sE"], 1)
                        for qb in range(QBn):
                            sc.wait_ge(S["sAv"], p * 128 + (h * 4 + qb) * 16 + 16)
                            i = sc.activation(out=A1[64:65, 8192 + qb * 512:8192 + qb * 512 + 512],
                                              in_=pb[4 + qb][64:65, 0:512], func=AF.Copy, bias=0.0, scale=1.0)
                            if qb == QBn - 1:
                                ssy(i, sc)
                        ssy(sc.activation(out=A1[64:65, 6144:8192], in_=A1[64:65, 8192:10240], func=AF.Ln, bias=0.0, scale=1.0), sc)
                        sc.activation(out=A1[64:65, 8192:10240], in_=A1[64:65, 6144:8192], func=AF.Exp, bias=0.0, scale=-1.0).then_inc(S["sLl"], 1)
                        for qb in range(QBn):
                            sc.wait_ge(S["sRp"], 8 * p + 4 * h + qb + 1)
                            if 8 * p + 4 * h + qb - 1 > 0:
                                sc.wait_ge(S["sAT"], 8 * p + 4 * h + qb - 1)
                            sc.activation(out=lrep[0:64, (qb % 2) * 512:(qb % 2) * 512 + 512],
                                          in_=pb[2][0:64, 0:512], func=AF.Copy, bias=0.0, scale=1.0).then_inc(S["sRe"], 1)

                def _combine(v, h):
                    vsy(v.reduce_max(out=mqk[:, 8 * h:8 * h + 1], in_=bass.AP(cmax, 4 * (4 * h), [[64, 128], [4, 4]]), axis=AX), v)
                    vsy(v.reduce_max(out=mqk[:, 8 * h + 4:8 * h + 5], in_=bass.AP(cmax, 4 * (8 + 4 * h), [[64, 128], [4, 4]]), axis=AX), v)
                    vsy(v.tensor_tensor(out=rts[:, 0:1], in0=mqk[:, 8 * h:8 * h + 1],
                                        in1=mqk[:, 8 * h + 4:8 * h + 5], op=AluOpType.add), v)
                    v.tensor_scalar(out=ncs[:, 4 * h:4 * h + 1], in0=rts[:, 0:1],
                                    scalar1=-0.5, scalar2=None, op0=AluOpType.mult).then_inc(S["sC"], 1)

                @blk.vector
                def _(v, p=p):
                    for i4 in range(16):
                        v.wait_ge(S["sCb"], 16 * p + i4 + 1)
                        v.reduce_max(out=cmax[:, 4 * i4:4 * i4 + 1], in_=pb[i4 % 2][:, 0:512], axis=AX).then_inc(S["sCr"], 1)
                        if i4 == 11:
                            v.wait_ge(S["sCr"], 16 * p + 12)
                            _combine(v, 0)
                        if i4 == 15:
                            v.wait_ge(S["sCr"], 16 * p + 16)
                            _combine(v, 1)

                    for h in range(2):
                        for qb in range(QBn):
                            g3 = 8 * p + 4 * h + qb
                            v.wait_ge(S["sRe"], g3 + 1)
                            if g3 >= 2:
                                v.wait_ge(S["sAH1" if g3 % 2 else "sAH0"], 16 * (g3 // 2))
                            v.tensor_tensor(out=attnT[0:64, (g3 % 2) * 512:(g3 % 2) * 512 + 512],
                                            in0=pb[4 + qb][0:64, 0:512], in1=lrep[0:64, (qb % 2) * 512:(qb % 2) * 512 + 512],
                                            op=AluOpType.mult).then_inc(S["sAT"], 1)

        for sbk in range(SBn):
            with nc.Block() as blk:
                @blk.sync
                def _(sync, sbk=sbk):
                    for ttoff in range(TTS):
                        gt = sbk * 2 + ttoff
                        if sbk == 0 and ttoff == 0:
                            sync.wait_ge(S["sAH0"], 16 * 32)
                            sync.wait_ge(S["sAH1"], 16 * 32)
                        src = bass.AP(ascr, gt * 128, [[SEQ, 128], [128 * SEQ, 8], [1, 128]])
                        dst = bass.AP(aT, (gt % 2) * 1024, [[2048, 128], [1, 1024]])
                        sync.dma_start(out=dst, in_=src).then_inc(S["sAg1" if gt % 2 else "sAg0"], 16)
                        sync.dma_start(out=A1[:, 8192 + (gt % 2) * 1024:8192 + (gt % 2) * 1024 + 1024],
                                       in_=bass.AP(x_in, gt * 128 * 1024, [[1024, 128], [1, 1024]])).then_inc(S["sX31" if gt % 2 else "sX30"], 16)
                    for fc in range(FC):
                        gfc = sbk * 32 + fc
                        if gfc - 1 > 0:
                            sync.wait_ge(S["sF1"], gfc - 1)
                        src = bass.AP(w1_d, fc * D * 128, [[128, 128], [128 * 128, 8], [1, 128]])
                        dst = bass.AP(w1s, (fc % 2) * 1024, [[2048, 128], [128, 8], [1, 128]])
                        sync.dma_start(out=dst, in_=src).then_inc(S["sW11" if fc % 2 else "sW10"], 16)
                    for fc in range(FC):
                        gfc = sbk * 32 + fc
                        if gfc - 1 > 0:
                            sync.wait_ge(S["sF2c"], gfc - 1)
                        src = bass.AP(w2_d, fc * 128 * 1024, [[1024, 128], [1, 1024]])
                        dst = bass.AP(w2s, (fc % 2) * 1024, [[2048, 128], [1, 1024]])
                        sync.dma_start(out=dst, in_=src).then_inc(S["sW21" if fc % 2 else "sW20"], 16)
                    if debug and sbk == 0:
                        sync.wait_ge(S["sRd"], 2)
                        sync.dma_start(out=bass.AP(dbg["d_x1"], 0, [[2048, 128], [1, 2048]]), in_=A2[:, 2048:4096]).then_inc(S["sOut"], 16)
                        sync.wait_ge(S["sTe3"], 4)
                        sync.dma_start(out=bass.AP(dbg["d_ln2T"], 0, [[2048, 128], [1, 2048]]), in_=A2[:, 0:2048]).then_inc(S["sOut"], 16)
                        sync.wait_ge(S["sGe3"], 1)
                        sync.dma_start(out=bass.AP(dbg["d_gT"], 0, [[512, 128], [1, 512]]), in_=A1[:, 0:512]).then_inc(S["sOut"], 16)
                        sync.wait_ge(S["sG"], 1)
                        sync.dma_start(out=bass.AP(dbg["d_rt"], 0, [[48, 128], [1, 8]]), in_=probs[:]).then_inc(S["sOut"], 16)
                        sync.dma_start(out=bass.AP(dbg["d_rt"], 8, [[48, 128], [1, 8]]), in_=nmax[:]).then_inc(S["sOut"], 16)
                        sync.dma_start(out=bass.AP(dbg["d_rt"], 16, [[48, 128], [1, 32]]), in_=rts[:]).then_inc(S["sOut"], 16)
                        sync.wait_ge(S["sG"], 2)
                        sync.dma_start(out=bass.AP(dbg["d_sgate"], 0, [[8, 128], [1, 8]]), in_=s_sb[:]).then_inc(S["sOut"], 16)
                    for ttoff in range(TTS):
                        gt = sbk * 2 + ttoff
                        sync.wait_ge(S["sX2"], gt + 1)
                        sync.dma_start(out=bass.AP(out_d, gt * 128 * 1024, [[1024, 128], [1, 1024]]),
                                       in_=x2b[:]).then_inc(S["sOut"], 16)
                    if sbk == SBn - 1:
                        sync.wait_ge(S["sOut"], 16 * 16 + (16 * 16 if debug else 0))

                @blk.tensor
                def _(tensor, sbk=sbk):
                    for ttoff in range(TTS):
                        gt = sbk * 2 + ttoff
                        tensor.wait_ge(S["sAg1" if gt % 2 else "sAg0"], 16 * (gt // 2 + 1))
                        tensor.wait_ge(S["sWoD"], 16)
                        if ttoff == 1:
                            tensor.wait_ge(S["sRd"], gt)
                        for dob in range(2):
                            for pr in range(8):
                                i = tensor.matmul(pb[dob][:, 0:512],
                                                  aT[:, (gt % 2) * 1024 + pr * 128:(gt % 2) * 1024 + pr * 128 + 128],
                                                  A2[:, 4096 + pr * 1024 + dob * 512:4096 + pr * 1024 + dob * 512 + 512],
                                                  start=(pr == 0), stop=(pr == 7))
                            i.then_inc(S["sY"], 1)
                    for ttoff in range(TTS):
                        gt = sbk * 2 + ttoff
                        tensor.wait_ge(S["sN3"], gt + 1)
                        if ttoff == 1:
                            tensor.wait_ge(S["sTe3"], 2 * gt)
                        for di in range(DI):
                            bank = 6 + di // 4
                            i = tensor.transpose(pb[bank][:, (di % 4) * 128:(di % 4) * 128 + 128],
                                                 nrm[:, di * 128:di * 128 + 128], ident[:])
                            if di == 3 or di == 7:
                                i.then_inc(S["sTr3"], 1)
                        tensor.wait_ge(S["sTe3"], 2 * gt + 2)
                        if gt == 0:
                            tensor.wait_ge(S["sRw"], 16)
                        if gt > 0:
                            tensor.wait_ge(S["sRe3"], gt)
                        for di in range(DI):
                            i = tensor.matmul(pb[2][:, 0:8], A2[:, di * 256 + ttoff * 128:di * 256 + ttoff * 128 + 128],
                                              rws[:, di * 8:di * 8 + 8], start=(di == 0), stop=(di == DI - 1))
                        i.then_inc(S["sRt"], 1)
                    tensor.wait_ge(S["sTe3"], 4 * sbk + 4)
                    for fc in range(FC):
                        gfc = sbk * 32 + fc
                        tensor.wait_ge(S["sW11" if fc % 2 else "sW10"], 16 * (gfc // 2 + 1))
                        if gfc - 1 > 0:
                            tensor.wait_ge(S["sGe3"], gfc - 1)
                        for di in range(DI):
                            i = tensor.matmul(pb[4 + fc % 2][:, 0:256],
                                              w1s[:, (fc % 2) * 1024 + di * 128:(fc % 2) * 1024 + di * 128 + 128],
                                              A2[:, di * 256:di * 256 + 256], start=(di == 0), stop=(di == DI - 1))
                        i.then_inc(S["sF1"], 1)
                    tensor.wait_ge(S["sRd"], 2 * sbk + 2)
                    tensor.wait_ge(S["sRe3"], 2 * sbk + 2)
                    for fc in range(FC):
                        gfc = sbk * 32 + fc
                        tensor.wait_ge(S["sW21" if fc % 2 else "sW20"], 16 * (gfc // 2 + 1))
                        tensor.wait_ge(S["sGe3"], gfc + 1)
                        for ttoff in range(TTS):
                            for dob in range(2):
                                i = tensor.matmul(pb[ttoff * 2 + dob][:, 0:512],
                                                  A1[:, fc * 256 + ttoff * 128:fc * 256 + ttoff * 128 + 128],
                                                  w2s[:, (fc % 2) * 1024 + dob * 512:(fc % 2) * 1024 + dob * 512 + 512],
                                                  start=(fc == 0), stop=(fc == FC - 1))
                        i.then_inc(S["sF2c"], 1)

                @blk.scalar
                def _(sc, sbk=sbk):
                    for ttoff in range(TTS):
                        gt = sbk * 2 + ttoff
                        sc.wait_ge(S["sRd"], gt + 1)
                        if gt >= 1:
                            sc.wait_ge(S["sSq3"], gt)
                        sc.activation(out=zb[:], in_=A2[:, 2048 + ttoff * 1024:2048 + ttoff * 1024 + 1024],
                                      func=AF.Square, bias=0.0, scale=1.0,
                                      accum_out=st2[:, 8 + 4 * ttoff:8 + 4 * ttoff + 1]).then_inc(S["sSq3"], 1)
                    sc.wait_ge(S["sVa3"], sbk + 1)
                    ssy(sc.activation(out=st2[:, 56:64], in_=st2[:, 48:56], func=AF.Ln, bias=0.0, scale=1.0), sc)
                    sc.activation(out=st2[:, 64:72], in_=st2[:, 56:64], func=AF.Exp, bias=0.0, scale=-0.5).then_inc(S["sSc3"], 1)
                    for ttoff in range(TTS):
                        gt = sbk * 2 + ttoff
                        for half in range(2):
                            sc.wait_ge(S["sTr3"], 2 * gt + half + 1)
                            outap = bass.AP(A2, half * 4 * 256 + ttoff * 128, [[16384, 128], [256, 4], [1, 128]])
                            inap = bass.AP(pb[6 + half], 0, [[512, 128], [128, 4], [1, 128]])
                            sc.activation(out=outap, in_=inap, func=AF.Copy, bias=0.0, scale=1.0).then_inc(S["sTe3"], 1)
                        sc.wait_ge(S["sRm"], gt + 1)
                        if gt > 0:
                            sc.wait_ge(S["sG"], gt)
                        sc.activation(out=probs[:], in_=pb[2][:, 0:8], func=AF.Exp,
                                      bias=nmax[:, 4 * ttoff:4 * ttoff + 1], scale=1.0).then_inc(S["sRe3"], 1)
                    if sbk == 0:
                        sc.wait_ge(S["sB1"], 16)
                    for fc in range(FC):
                        gfc = sbk * 32 + fc
                        sc.wait_ge(S["sF1"], gfc + 1)
                        sc.activation(out=A1[:, fc * 256:fc * 256 + 256], in_=pb[4 + fc % 2][:, 0:256],
                                      func=AF.Gelu_apprx_tanh, bias=b1s[:, 4 * fc:4 * fc + 1],
                                      scale=1.0).then_inc(S["sGe3"], 1)

                @blk.vector
                def _(v, sbk=sbk):
                    for ttoff in range(TTS):
                        gt = sbk * 2 + ttoff
                        v.wait_ge(S["sX31" if gt % 2 else "sX30"], 16 * (gt // 2 + 1))
                        for dob in range(2):
                            v.wait_ge(S["sY"], gt * 2 + dob + 1)
                            vsy(v.tensor_tensor(out=A2[:, 2048 + ttoff * 1024 + dob * 512:2048 + ttoff * 1024 + dob * 512 + 512],
                                                in0=pb[dob][:, 0:512],
                                                in1=A1[:, 8192 + (gt % 2) * 1024 + dob * 512:8192 + (gt % 2) * 1024 + dob * 512 + 512],
                                                op=AluOpType.add), v)
                        v.tensor_reduce(out=st2[:, 4 * ttoff:4 * ttoff + 1],
                                        in_=A2[:, 2048 + ttoff * 1024:2048 + ttoff * 1024 + 1024],
                                        axis=AX, op=AluOpType.add).then_inc(S["sRd"], 1)
                    v.wait_ge(S["sSq3"], 2 * sbk + 2)
                    v.wait_ge(S["sRd"], 2 * sbk + 2)
                    vsy(v.tensor_scalar(out=st2[:, 16:24], in0=st2[:, 0:8], scalar1=1.0 / D, scalar2=None, op0=AluOpType.mult), v)
                    vsy(v.tensor_scalar(out=st2[:, 24:32], in0=st2[:, 8:16], scalar1=1.0 / D, scalar2=None, op0=AluOpType.mult), v)
                    vsy(v.scalar_tensor_tensor(out=st2[:, 32:40], in0=st2[:, 16:24], scalar=-1.0,
                                               in1=st2[:, 16:24], op0=AluOpType.mult, op1=AluOpType.mult), v)
                    vsy(v.tensor_tensor(out=st2[:, 40:48], in0=st2[:, 24:32], in1=st2[:, 32:40], op=AluOpType.add), v)
                    v.tensor_scalar(out=st2[:, 48:56], in0=st2[:, 40:48], scalar1=1e-5, scalar2=None, op0=AluOpType.add).then_inc(S["sVa3"], 1)
                    v.wait_ge(S["sSc3"], sbk + 1)
                    for ttoff in range(TTS):
                        gt = sbk * 2 + ttoff
                        if ttoff == 1:
                            v.wait_ge(S["sTr3"], 2 * gt)
                        v.tensor_scalar(out=nrm[:], in0=A2[:, 2048 + ttoff * 1024:2048 + ttoff * 1024 + 1024],
                                        scalar1=st2[:, 16 + 4 * ttoff:16 + 4 * ttoff + 1],
                                        scalar2=st2[:, 64 + 4 * ttoff:64 + 4 * ttoff + 1],
                                        op0=AluOpType.subtract, op1=AluOpType.mult).then_inc(S["sN3"], 1)
                        v.wait_ge(S["sRt"], gt + 1)
                        v.reduce_max(out=nmax[:, 4 * ttoff:4 * ttoff + 1], in_=pb[2][:, 0:8], axis=AX, negate=True).then_inc(S["sRm"], 1)
                        v.wait_ge(S["sRe3"], gt + 1)
                        vsy(v.reduce_max(out=rts[:, 0:1], in_=probs[:], axis=AX), v)
                        vsy(v.tensor_scalar(out=lts[:], in0=probs[:], scalar1=rts[:, 0:1], scalar2=None, op0=AluOpType.is_lt), v)
                        vsy(v.tensor_tensor(out=msk[:], in0=probs[:], in1=lts[:], op=AluOpType.mult), v)
                        vsy(v.reduce_max(out=rts[:, 4:5], in_=msk[:], axis=AX), v)
                        vsy(v.tensor_tensor(out=rts[:, 8:9], in0=rts[:, 0:1], in1=rts[:, 4:5], op=AluOpType.add), v)
                        vsy(v.reciprocal(out=rts[:, 12:13], in_=rts[:, 8:9]), v)
                        vsy(v.tensor_tensor(out=rts[:, 16:17], in0=rts[:, 0:1], in1=rts[:, 12:13], op=AluOpType.mult), v)
                        vsy(v.tensor_tensor(out=rts[:, 20:21], in0=rts[:, 4:5], in1=rts[:, 12:13], op=AluOpType.mult), v)
                        v.tensor_tensor(out=s_sb[:, 4 * ttoff:4 * ttoff + 1], in0=rts[:, 16:17], in1=rts[:, 20:21],
                                        op=AluOpType.add).then_inc(S["sG"], 1)
                    if sbk == 0:
                        v.wait_ge(S["sB2r"], 16)
                    v.wait_ge(S["sF2c"], 32 * (sbk + 1))
                    v.wait_ge(S["sG"], 2 * sbk + 2)
                    for ttoff in range(TTS):
                        gt = sbk * 2 + ttoff
                        if gt > 0:
                            v.wait_ge(S["sOut"], 16 * gt)
                        for dob in range(2):
                            vsy(v.tensor_tensor(out=zb[:, dob * 512:dob * 512 + 512], in0=pb[ttoff * 2 + dob][:, 0:512],
                                                in1=b2rep[:, dob * 512:dob * 512 + 512], op=AluOpType.add), v)
                        v.scalar_tensor_tensor(out=x2b[:], in0=zb[:], scalar=s_sb[:, 4 * ttoff:4 * ttoff + 1],
                                               in1=A2[:, 2048 + ttoff * 1024:2048 + ttoff * 1024 + 1024],
                                               op0=AluOpType.mult, op1=AluOpType.add).then_inc(S["sX2"], 1)

    return nc


_NC_CACHE = {}


def kernel(L0_attn_x, L0_attn_gamma, L0_attn_w_qkv, L0_attn_w_out,
           L0_ffn_router_w, L0_ffn_w1, L0_ffn_b1, L0_ffn_w2, L0_ffn_b2):
    x = np.ascontiguousarray(np.asarray(L0_attn_x, dtype=np.float32))
    gam = np.ascontiguousarray(np.asarray(L0_attn_gamma, dtype=np.float32))
    wqkv = np.ascontiguousarray(np.asarray(L0_attn_w_qkv, dtype=np.float32))
    wout = np.ascontiguousarray(np.asarray(L0_attn_w_out, dtype=np.float32))
    rw = np.ascontiguousarray(np.asarray(L0_ffn_router_w, dtype=np.float32))
    w1 = np.ascontiguousarray(np.asarray(L0_ffn_w1, dtype=np.float32))
    b1 = np.ascontiguousarray(np.asarray(L0_ffn_b1, dtype=np.float32))
    w2 = np.ascontiguousarray(np.asarray(L0_ffn_w2, dtype=np.float32))
    b2 = np.ascontiguousarray(np.asarray(L0_ffn_b2, dtype=np.float32))

    b1l = np.zeros((128, 128), dtype=np.float32)
    for fc in range(32):
        b1l[:, 4 * fc] = b1[fc * 128:(fc + 1) * 128]
    b2rep = np.ascontiguousarray(np.broadcast_to(b2[None, :], (128, D)))
    rwl = np.ascontiguousarray(rw.reshape(8, 128, 8).transpose(1, 0, 2).reshape(128, 64))
    ident = np.eye(128, dtype=np.float32)

    if "nc" not in _NC_CACHE:
        _NC_CACHE["nc"] = build_nc()
    nc = _NC_CACHE["nc"]

    wqkvp = np.ascontiguousarray(wqkv.reshape(D, 3, 8, 128).transpose(2, 0, 1, 3).reshape(8, D, 384))
    w1p = np.ascontiguousarray(w1.reshape(D, 32, 128).transpose(1, 0, 2))
    common = {"wqkv": wqkvp, "wout": wout, "w1": w1p, "w2": w2, "gam": gam,
              "b1l": b1l, "b2rep": b2rep, "rwl": rwl, "ident": ident}
    in_maps = [dict(common, x_in=np.ascontiguousarray(x[b])) for b in range(8)]
    res = run_bass_kernel_spmd(nc, in_maps, core_ids=list(range(8)))
    return np.stack([np.asarray(res.results[i]["out_x"]) for i in range(8)], axis=0)


if __name__ == "__main__":
    rng = np.random.default_rng(0)
    ins = {
        "L0_attn_x": rng.standard_normal((8, SEQ, D)).astype(np.float32),
        "L0_attn_gamma": np.zeros(D, np.float32),
        "L0_attn_w_qkv": (rng.standard_normal((D, 3 * D)) * D ** -0.5).astype(np.float32),
        "L0_attn_w_out": (rng.standard_normal((D, D)) * D ** -0.5).astype(np.float32),
        "L0_ffn_router_w": (rng.standard_normal((D, 8)) * 0.02).astype(np.float32),
        "L0_ffn_w1": (rng.standard_normal((D, 4 * D)) * 0.02).astype(np.float32),
        "L0_ffn_b1": np.zeros(4 * D, np.float32),
        "L0_ffn_w2": (rng.standard_normal((4 * D, D)) * 0.02).astype(np.float32),
        "L0_ffn_b2": np.zeros(D, np.float32),
    }
    out = kernel(**ins)
    print("out", out.shape, out.dtype, np.abs(out).mean())


# version 05 (final)
# speedup vs baseline: 1.3052x; 1.3052x over previous
import numpy as np
import concourse.bass as bass
import concourse.mybir as mybir
from concourse.bass_utils import run_bass_kernel_spmd
from concourse.alu_op_type import AluOpType
from contextlib import ExitStack

F32 = mybir.dt.float32
BF16 = mybir.dt.bfloat16
AF = mybir.ActivationFunctionType
AX = mybir.AxisListType.X

D = 1024
SEQ = 2048
NT = 16
DI = 8
PAIRS = 8
QBn = 4
KC = 16
SBn = 8
TTS = 2
FC = 32
QS = 0.35355339059327373


def build_nc(debug=False):
    nc = bass.Bass("TRN2", target_bir_lowering=False, detect_race_conditions=False)
    dbg = {}
    if debug:
        for nm, shp in [("d_gfrep", [128, 1024]), ("d_stats", [128, 1024]),
                        ("d_xnT", [128, 2048]), ("d_ncs", [128, 8]),
                        ("d_x1", [128, 2048]), ("d_ln2T", [128, 2048]),
                        ("d_gT", [128, 512]), ("d_sgate", [128, 8]), ("d_rt", [128, 48])]:
            dbg[nm] = nc.dram_tensor(nm, shp, F32, kind="ExternalOutput")

    x_in = nc.dram_tensor("x_in", [SEQ, D], F32, kind="ExternalInput")
    wqkv = nc.dram_tensor("wqkv", [8, D, 384], F32, kind="ExternalInput")
    wout_d = nc.dram_tensor("wout", [D, D], F32, kind="ExternalInput")
    w1_d = nc.dram_tensor("w1", [32, D, 128], F32, kind="ExternalInput")
    w2_d = nc.dram_tensor("w2", [4 * D, D], F32, kind="ExternalInput")
    gam_d = nc.dram_tensor("gam", [D], F32, kind="ExternalInput")
    b1_d = nc.dram_tensor("b1l", [128, 128], F32, kind="ExternalInput")
    b2_d = nc.dram_tensor("b2rep", [128, D], F32, kind="ExternalInput")
    rw_d = nc.dram_tensor("rwl", [128, 64], F32, kind="ExternalInput")
    id_d = nc.dram_tensor("ident", [128, 128], F32, kind="ExternalInput")
    ascr = nc.dram_tensor("attn_scratch", [D, SEQ], F32)
    out_d = nc.dram_tensor("out_x", [SEQ, D], F32, kind="ExternalOutput")

    with ExitStack() as st:
        def sb(name, shape, dt=F32):
            return st.enter_context(nc.sbuf_tensor(name, shape, dt))

        A1 = sb("arena1", [128, 10240])
        A2 = sb("arena2", [128, 16384])
        vtok = sb("vtok", [128, 16 * 136])
        qkbf = sb("qkbf", [128, 8192], BF16)
        attnT = sb("attnT", [128, 1024])
        wsl = sb("wsl", [128, 6144])
        nrm = sb("nrm", [128, 1024])
        aT = sb("aTb", [128, 2048])
        zb = sb("zb", [128, 1024])
        x2b = sb("x2b", [128, 1024])
        b2rep = sb("b2r", [128, 1024])
        ident = sb("idents", [128, 128])
        ones = sb("onesf", [128, 128])
        b1s = sb("b1s", [128, 128])
        rws = sb("rws", [128, 64])
        stats = sb("stats", [128, 1024])
        st2 = sb("st2", [128, 128])
        ncs = sb("ncs", [128, 8])
        cmax = sb("cmaxs", [128, 64])
        mqk = sb("mqks", [128, 16])
        nmax = sb("nmaxs", [128, 8])
        probs = sb("probss", [128, 8])
        lts = sb("lts", [128, 8])
        msk = sb("msks", [128, 8])
        rts = sb("rts", [128, 32])
        s_sb = sb("s_sbs", [128, 8])
        sqbf = sb("sqbf", [128, 2048], BF16)
        onesbf = sb("onesbf", [128, 128], BF16)

        pb = [st.enter_context(nc.psum_tensor("pb%d" % i, [128, 512], F32)) for i in range(8)]

        SEMS = ["sGam", "sB1", "sB2r", "sRw", "sId", "sMs", "sGf", "sGfe",
                "sXA0", "sXA1", "sSq1", "sSm1", "sVa1", "sSc1",
                "sR1s", "sXB0", "sXB1", "sN1", "sTr1", "sTe1", "sWd0", "sWd1", "sQg", "sQe",
                "sVt", "sVe", "sSqk", "sCb", "sCr", "sQeV", "sC", "sS", "sE", "sAv", "sLl",
                "sRp", "sRe", "sAT", "sAH0", "sAH1", "sWoD", "sAg0", "sAg1",
                "sX30", "sX31", "sW10", "sW11", "sW20", "sW21", "sY",
                "sRd", "sSq3", "sVa3", "sSc3", "sN3", "sTr3", "sTe3", "sRt", "sRm",
                "sRe3", "sG", "sF1", "sGe3", "sF2c", "sX2", "sOut", "sVr", "sSr"]
        S = {n: st.enter_context(nc.semaphore(n)) for n in SEMS}

        P_sl = lambda psl: A1[:, psl * 512:psl * 512 + 512]
        vT = lambda lo, n: A1[:, 2048 + lo:2048 + lo + n]

        _vr = [0]
        def vsy(instr, v):
            _vr[0] += 1
            instr.then_inc(S["sVr"], 1)
            v.wait_ge(S["sVr"], _vr[0])
        _sr = [0]
        def ssy(instr, sc):
            _sr[0] += 1
            instr.then_inc(S["sSr"], 1)
            sc.wait_ge(S["sSr"], _sr[0])

        with nc.Block() as blk:
            @blk.sync
            def _(sync):
                sync.dma_start(out=zb[0:1, 0:1024], in_=bass.AP(gam_d, 0, [[1024, 1], [1, 1024]])).then_inc(S["sGam"], 16)
                sync.dma_start(out=b1s[:], in_=bass.AP(b1_d, 0, [[128, 128], [1, 128]])).then_inc(S["sB1"], 16)
                sync.dma_start(out=rws[:], in_=bass.AP(rw_d, 0, [[64, 128], [1, 64]])).then_inc(S["sRw"], 16)
                sync.dma_start(out=ident[:], in_=bass.AP(id_d, 0, [[128, 128], [1, 128]])).then_inc(S["sId"], 16)
                src = bass.AP(wqkv, 0 * D * 384, [[384, 128], [128 * 384, 8], [1, 384]])
                dst = bass.AP(wsl, 0 * 3072, [[6144, 128], [384, 8], [1, 384]])
                sync.dma_start(out=dst, in_=src).then_inc(S["sWd0"], 16)
                for t in range(NT):
                    if t >= 2:
                        sync.wait_ge(S["sSq1"], t - 1)
                        sync.wait_ge(S["sSm1"], t - 1)
                    sync.dma_start(out=aT[:, (t % 2) * 1024:(t % 2) * 1024 + 1024],
                                   in_=bass.AP(x_in, t * 128 * 1024, [[1024, 128], [1, 1024]])).then_inc(S["sXA1" if t % 2 else "sXA0"], 16)
                for t in range(NT):
                    if t >= 2:
                        sync.wait_ge(S["sN1"], t - 1)
                    else:
                        sync.wait_ge(S["sSq1"], 15 + t)
                        sync.wait_ge(S["sSm1"], 15 + t)
                    sync.dma_start(out=aT[:, (t % 2) * 1024:(t % 2) * 1024 + 1024],
                                   in_=bass.AP(x_in, t * 128 * 1024, [[1024, 128], [1, 1024]])).then_inc(S["sXB1" if t % 2 else "sXB0"], 16)

            @blk.gpsimd
            def _(g):
                g.memset(ones[:], 1.0)
                g.memset(stats[:], 1.0)
                g.memset(st2[:], 1.0)
                g.memset(bass.AP(vtok, 64, [[16 * 136, 128], [136, 16]]), 1.0)
                g.memset(bass.AP(vtok, 132, [[16 * 136, 128], [136, 16]]), 1.0).then_inc(S["sMs"], 1)

            @blk.tensor
            def _(tensor):
                tensor.wait_ge(S["sGam"], 16)
                tensor.wait_ge(S["sMs"], 1)
                tensor.matmul(pb[0][:, 0:512], ones[0:1, 0:128], zb[0:1, 0:512], start=True, stop=True)
                tensor.matmul(pb[1][:, 0:512], ones[0:1, 0:128], zb[0:1, 512:1024], start=True, stop=True).then_inc(S["sGf"], 1)
                tensor.wait_ge(S["sId"], 16)
                for t in range(NT):
                    tensor.wait_ge(S["sN1"], t + 1)
                    if t >= 1:
                        tensor.wait_ge(S["sTe1"], 2 * (t - 1) + 2)
                    for di in range(DI):
                        bank = 6 + di // 4
                        i = tensor.transpose(pb[bank][:, (di % 4) * 128:(di % 4) * 128 + 128],
                                             nrm[:, di * 128:di * 128 + 128], ident[:])
                        if di == 3 or di == 7:
                            i.then_inc(S["sTr1"], 1)

            @blk.scalar
            def _(sc):
                sc.wait_ge(S["sMs"], 1)
                sc.wait_ge(S["sGf"], 1)
                sc.activation(out=x2b[:, 0:512], in_=pb[0][:, 0:512], func=AF.Copy, bias=1.0, scale=1.0)
                sc.activation(out=x2b[:, 512:1024], in_=pb[1][:, 0:512], func=AF.Copy, bias=1.0, scale=1.0).then_inc(S["sGfe"], 1)
                sc.wait_ge(S["sMs"], 1)
                sc.activation(out=onesbf[:], in_=ones[:], func=AF.Copy, bias=0.0, scale=1.0)
                for t in range(NT):
                    sc.wait_ge(S["sXA1" if t % 2 else "sXA0"], 16 * (t // 2 + 1))
                    if t >= 1:
                        sc.wait_ge(S["sSq1"], t)
                    sc.activation(out=zb[:], in_=aT[:, (t % 2) * 1024:(t % 2) * 1024 + 1024],
                                  func=AF.Square, bias=0.0, scale=1.0,
                                  accum_out=stats[:, 64 + 4 * t:64 + 4 * t + 1]).then_inc(S["sSq1"], 1)
                sc.wait_ge(S["sVa1"], 1)
                ssy(sc.activation(out=stats[:, 576:640], in_=stats[:, 384:448], func=AF.Ln, bias=0.0, scale=1.0), sc)
                ssy(sc.activation(out=stats[:, 640:704], in_=stats[:, 576:640], func=AF.Exp, bias=0.0, scale=-0.5), sc)
                ssy(sc.activation(out=stats[:, 576:640], in_=stats[:, 512:576], func=AF.Ln, bias=0.0, scale=1.0), sc)
                sc.activation(out=stats[:, 704:768], in_=stats[:, 576:640], func=AF.Exp, bias=0.0, scale=0.5).then_inc(S["sSc1"], 1)
                for t in range(NT):
                    for half in range(2):
                        sc.wait_ge(S["sTr1"], 2 * t + half + 1)
                        outap = bass.AP(A2, half * 4 * 2048 + t * 128, [[16384, 128], [2048, 4], [1, 128]])
                        inap = bass.AP(pb[6 + half], 0, [[512, 128], [128, 4], [1, 128]])
                        sc.activation(out=outap, in_=inap, func=AF.Copy, bias=0.0, scale=1.0).then_inc(S["sTe1"], 1)

            @blk.vector
            def _(v):
                v.wait_ge(S["sMs"], 1)
                for t in range(NT):
                    v.wait_ge(S["sXA1" if t % 2 else "sXA0"], 16 * (t // 2 + 1))
                    v.tensor_reduce(out=stats[:, 4 * t:4 * t + 1], in_=aT[:, (t % 2) * 1024:(t % 2) * 1024 + 1024],
                                    axis=AX, op=AluOpType.add).then_inc(S["sSm1"], 1)
                v.wait_ge(S["sSq1"], 16)
                v.wait_ge(S["sSm1"], 16)
                vsy(v.tensor_scalar(out=stats[:, 128:192], in0=stats[:, 0:64], scalar1=1.0 / D, scalar2=None, op0=AluOpType.mult), v)
                vsy(v.tensor_scalar(out=stats[:, 192:256], in0=stats[:, 64:128], scalar1=1.0 / D, scalar2=None, op0=AluOpType.mult), v)
                vsy(v.scalar_tensor_tensor(out=stats[:, 256:320], in0=stats[:, 128:192], scalar=-1.0,
                                           in1=stats[:, 128:192], op0=AluOpType.mult, op1=AluOpType.mult), v)
                vsy(v.tensor_tensor(out=stats[:, 320:384], in0=stats[:, 192:256], in1=stats[:, 256:320], op=AluOpType.add), v)
                vsy(v.tensor_scalar(out=stats[:, 384:448], in0=stats[:, 320:384], scalar1=1e-5, scalar2=None, op0=AluOpType.add), v)
                vsy(v.reciprocal(out=stats[:, 448:512], in_=stats[:, 384:448]), v)
                v.scalar_tensor_tensor(out=stats[:, 512:576], in0=stats[:, 320:384], scalar=float(D),
                                       in1=stats[:, 448:512], op0=AluOpType.mult, op1=AluOpType.mult).then_inc(S["sVa1"], 1)
                v.wait_ge(S["sSc1"], 1)
                vsy(v.tensor_scalar(out=stats[:, 768:832], in0=stats[:, 704:768], scalar1=1e-12, scalar2=None, op0=AluOpType.max), v)
                vsy(v.reciprocal(out=stats[:, 832:896], in_=stats[:, 768:832]), v)
                v.scalar_tensor_tensor(out=stats[:, 896:960], in0=stats[:, 640:704], scalar=32.0,
                                       in1=stats[:, 832:896], op0=AluOpType.mult, op1=AluOpType.mult).then_inc(S["sR1s"], 1)
                v.wait_ge(S["sGfe"], 1)
                for t in range(NT):
                    v.wait_ge(S["sXB1" if t % 2 else "sXB0"], 16 * (t // 2 + 1))
                    if t >= 1:
                        v.wait_ge(S["sTr1"], 2 * t)
                    v.tensor_scalar(out=zb[:], in0=aT[:, (t % 2) * 1024:(t % 2) * 1024 + 1024],
                                    scalar1=stats[:, 128 + 4 * t:128 + 4 * t + 1], scalar2=stats[:, 896 + 4 * t:896 + 4 * t + 1],
                                    op0=AluOpType.subtract, op1=AluOpType.mult)
                    v.tensor_tensor(out=nrm[:], in0=zb[:], in1=x2b[:], op=AluOpType.mult).then_inc(S["sN1"], 1)

        for p in range(PAIRS):
            with nc.Block() as blk:
                @blk.sync
                def _(sync, p=p):
                    if p + 1 < PAIRS:
                        sl = (p + 1) % 2
                        sem = S["sWd1"] if sl == 1 else S["sWd0"]
                        src = bass.AP(wqkv, (p + 1) * D * 384, [[384, 128], [128 * 384, 8], [1, 384]])
                        dst = bass.AP(wsl, sl * 3072, [[6144, 128], [384, 8], [1, 384]])
                        sync.dma_start(out=dst, in_=src).then_inc(sem, 16)
                    for qb in range(QBn):
                        for h in range(2):
                            G = p * 8 + qb * 2 + h
                            sync.wait_ge(S["sAT"], G + 1)
                            dst = bass.AP(ascr, (p * 128 + h * 64) * SEQ + qb * 512, [[SEQ, 64], [1, 512]])
                            sync.dma_start(out=dst, in_=attnT[0:64, (G % 2) * 512:(G % 2) * 512 + 512]).then_inc(S["sAH1" if G % 2 else "sAH0"], 16)
                    if debug and p == 0:
                        sync.wait_ge(S["sGfe"], 1)
                        sync.dma_start(out=bass.AP(dbg["d_gfrep"], 0, [[1024, 128], [1, 1024]]), in_=x2b[:]).then_inc(S["sOut"], 16)
                        sync.wait_ge(S["sR1s"], 1)
                        sync.dma_start(out=bass.AP(dbg["d_stats"], 0, [[1024, 128], [1, 1024]]), in_=stats[:]).then_inc(S["sOut"], 16)
                        sync.dma_start(out=bass.AP(dbg["d_xnT"], 0, [[2048, 128], [1, 2048]]), in_=A2[:, 0:2048]).then_inc(S["sOut"], 16)
                        sync.wait_ge(S["sC"], 2)
                        sync.dma_start(out=bass.AP(dbg["d_ncs"], 0, [[8, 128], [1, 8]]), in_=ncs[:]).then_inc(S["sOut"], 16)
                    if p == PAIRS - 1:
                        sync.wait_ge(S["sQg"], 96)
                        src = bass.AP(wout_d, 0, [[1024, 128], [128 * 1024, 8], [1, 1024]])
                        dst = bass.AP(A2, 4096, [[16384, 128], [1024, 8], [1, 1024]])
                        sync.dma_start(out=dst, in_=src).then_inc(S["sWoD"], 16)

                @blk.tensor
                def _(tensor, p=p):
                    wsem = S["sWd1"] if p % 2 == 1 else S["sWd0"]
                    tensor.wait_ge(wsem, 16 * (p // 2 + 1))
                    base = (p % 2) * 3072
                    for stn in range(3):
                        for qb in range(QBn):
                            g = stn * 4 + qb
                            if 12 * p + g - 1 > 0:
                                tensor.wait_ge(S["sQe"], 12 * p + g - 1)
                            if 2 <= g <= 9:
                                tensor.wait_ge(S["sQeV"], 8 * p + g - 1)
                            for di in range(DI):
                                i = tensor.matmul(pb[g % 2][:, 0:512],
                                                  wsl[:, base + di * 384 + stn * 128: base + di * 384 + stn * 128 + 128],
                                                  A2[:, di * 2048 + qb * 512: di * 2048 + qb * 512 + 512],
                                                  start=(di == 0), stop=(di == DI - 1))
                            i.then_inc(S["sQg"], 1)
                    tensor.wait_ge(S["sQe"], 12 * p + 12)
                    for r in range(4):
                        if r >= 2:
                            tensor.wait_ge(S["sVe"], 8 * p + 2 * r - 2)
                        for j in range(4):
                            tt = 4 * r + j
                            i = tensor.transpose(pb[6 + r % 2][:, j * 128:j * 128 + 128],
                                                 vT(tt * 128, 128), ident[:])
                        i.then_inc(S["sVt"], 1)
                    for src_i in range(2):
                        tensor.wait_ge(S["sSqk"], 2 * p + src_i + 1)
                        for h in range(2):
                            for blkq in range(4):
                                i4 = src_i * 8 + h * 4 + blkq
                                if 16 * p + i4 - 1 > 0:
                                    tensor.wait_ge(S["sCr"], 16 * p + i4 - 1)
                                tensor.matmul(pb[i4 % 2][:, 0:512], onesbf[h * 64:h * 64 + 64, 0:128],
                                              sqbf[h * 64:h * 64 + 64, blkq * 512:blkq * 512 + 512],
                                              start=True, stop=True).then_inc(S["sCb"], 1)
                    tensor.wait_ge(S["sVe"], 8 * p + 8)
                    tensor.wait_ge(S["sCr"], 16 * p + 16)
                    tensor.wait_ge(S["sQeV"], 8 * p + 8)
                    PREV_O = {2: 1, 3: 0, 4: 3, 5: 2, 6: 5, 7: 4}

                    def rep_mms(qb_r):
                        for h in range(2):
                            glp = qb_r * 2 + h
                            tensor.wait_ge(S["sLl"], p * 8 + glp + 1)
                            if glp >= 2:
                                tensor.wait_ge(S["sRe"], p * 8 + glp - 1)
                            tensor.matmul(pb[glp % 2][0:64, 0:512], ones[64:65, 0:64],
                                          A1[64:65, 4096 + (glp % 2) * 512:4096 + (glp % 2) * 512 + 512],
                                          start=True, stop=True).then_inc(S["sRp"], 1)

                    LAG = 2
                    for qb in range(QBn):
                        if qb >= 1:
                            rep_mms(qb - 1)
                        for h in range(2):
                            gl = qb * 2 + h
                            pv = PREV_O.get(gl)
                            if pv is not None:
                                tensor.wait_ge(S["sAT"], p * 8 + pv + 1)
                        for step in range(KC + LAG):
                            if step < KC:
                                kc = step
                                for h in range(2):
                                    sidx2 = p * 128 + (qb * 16 + kc) * 2 + h
                                    if sidx2 - 3 > 0:
                                        tensor.wait_ge(S["sE"], sidx2 - 3)
                                    bank = (2 + h) if kc % 2 == 0 else (6 + h)
                                    kh = qkbf[h * 64:h * 64 + 64, 4096 + kc * 128:4096 + kc * 128 + 128]
                                    kl = qkbf[h * 64:h * 64 + 64, 6144 + kc * 128:6144 + kc * 128 + 128]
                                    qh = qkbf[h * 64:h * 64 + 64, qb * 512:qb * 512 + 512]
                                    ql = qkbf[h * 64:h * 64 + 64, 2048 + qb * 512:2048 + qb * 512 + 512]
                                    tensor.matmul(pb[bank][:, 0:512], kh, qh, start=True, stop=False)
                                    tensor.matmul(pb[bank][:, 0:512], kh, ql, start=False, stop=False)
                                    tensor.matmul(pb[bank][:, 0:512], kl, qh,
                                                  start=False, stop=True).then_inc(S["sS"], 1)
                            if step >= LAG:
                                kc = step - LAG
                                for h in range(2):
                                    sidx2 = p * 128 + (qb * 16 + kc) * 2 + h
                                    tensor.wait_ge(S["sE"], sidx2 + 1)
                                    psl = 2 * h + kc % 2
                                    tensor.matmul(pb[4 + (qb + h) % 2][0:65, 0:512],
                                                  vtok[:, kc * 136 + 68 * h: kc * 136 + 68 * h + 65],
                                                  P_sl(psl), start=(kc == 0), stop=(kc == KC - 1)).then_inc(S["sAv"], 1)
                    rep_mms(QBn - 1)

                @blk.scalar
                def _(sc, p=p):
                    for stn in range(3):
                        for qb in range(QBn):
                            g = stn * 4 + qb
                            sc.wait_ge(S["sQg"], 12 * p + g + 1)
                            if stn == 0:
                                out = qkbf[:, qb * 512:qb * 512 + 512]
                                scale = QS
                            elif stn == 1:
                                out = qkbf[:, 4096 + qb * 512:4096 + qb * 512 + 512]
                                scale = QS
                            else:
                                out = A1[:, 2048 + qb * 512:2048 + qb * 512 + 512]
                                scale = 1.0
                            sc.activation(out=out, in_=pb[g % 2][:, 0:512],
                                          func=AF.Copy, bias=0.0, scale=scale).then_inc(S["sQe"], 1)
                    for r in range(4):
                        sc.wait_ge(S["sVt"], 4 * p + r + 1)
                        for part in range(2):
                            outap = bass.AP(vtok, (4 * r) * 136 + 68 * part, [[16 * 136, 128], [136, 4], [1, 64]])
                            inap = bass.AP(pb[6 + r % 2], part * 64, [[512, 128], [128, 4], [1, 64]])
                            sc.activation(out=outap, in_=inap, func=AF.Copy, bias=0.0, scale=1.0).then_inc(S["sVe"], 1)
                    sc.wait_ge(S["sQe"], 12 * p + 4)
                    sc.activation(out=sqbf[:], in_=qkbf[:, 0:2048], func=AF.Square, bias=0.0, scale=1.0).then_inc(S["sSqk"], 1)
                    sc.wait_ge(S["sCb"], 16 * p + 8)
                    sc.activation(out=sqbf[:], in_=qkbf[:, 4096:6144], func=AF.Square, bias=0.0, scale=1.0).then_inc(S["sSqk"], 1)
                    for qb in range(QBn):
                        for kc in range(KC):
                            for h in range(2):
                                sidx2 = p * 128 + (qb * 16 + kc) * 2 + h
                                sc.wait_ge(S["sS"], sidx2 + 1)
                                if kc == 0 and qb == 0:
                                    sc.wait_ge(S["sC"], 2 * p + h + 1)
                                if sidx2 - 3 > 0:
                                    sc.wait_ge(S["sAv"], sidx2 - 3)
                                bank = (2 + h) if kc % 2 == 0 else (6 + h)
                                sc.activation(out=P_sl(2 * h + kc % 2), in_=pb[bank][:, 0:512], func=AF.Exp,
                                              bias=ncs[:, 4 * h:4 * h + 1], scale=1.0).then_inc(S["sE"], 1)
                        for h in range(2):
                            gl = qb * 2 + h
                            lsl = (gl % 2) * 512
                            sc.wait_ge(S["sAv"], p * 128 + (qb * 16 + 15) * 2 + h + 1)
                            ssy(sc.activation(out=A1[64:65, 4096 + lsl:4096 + lsl + 512],
                                              in_=pb[4 + (qb + h) % 2][64:65, 0:512], func=AF.Copy, bias=0.0, scale=1.0), sc)
                            ssy(sc.activation(out=A1[64:65, 2048 + lsl:2048 + lsl + 512],
                                              in_=A1[64:65, 4096 + lsl:4096 + lsl + 512], func=AF.Ln, bias=0.0, scale=1.0), sc)
                            sc.activation(out=A1[64:65, 4096 + lsl:4096 + lsl + 512],
                                          in_=A1[64:65, 2048 + lsl:2048 + lsl + 512], func=AF.Exp,
                                          bias=0.0, scale=-1.0).then_inc(S["sLl"], 1)
                            sc.wait_ge(S["sRp"], 8 * p + gl + 1)
                            if 8 * p + gl - 1 > 0:
                                sc.wait_ge(S["sAT"], 8 * p + gl - 1)
                            sc.activation(out=b2rep[0:64, lsl:lsl + 512],
                                          in_=pb[gl % 2][0:64, 0:512], func=AF.Copy, bias=0.0, scale=1.0).then_inc(S["sRe"], 1)

                def _combine(v, h):
                    vsy(v.reduce_max(out=mqk[:, 8 * h:8 * h + 1], in_=bass.AP(cmax, 4 * (4 * h), [[64, 128], [4, 4]]), axis=AX), v)
                    vsy(v.reduce_max(out=mqk[:, 8 * h + 4:8 * h + 5], in_=bass.AP(cmax, 4 * (8 + 4 * h), [[64, 128], [4, 4]]), axis=AX), v)
                    vsy(v.tensor_tensor(out=rts[:, 0:1], in0=mqk[:, 8 * h:8 * h + 1],
                                        in1=mqk[:, 8 * h + 4:8 * h + 5], op=AluOpType.add), v)
                    v.tensor_scalar(out=ncs[:, 4 * h:4 * h + 1], in0=rts[:, 0:1],
                                    scalar1=-0.5, scalar2=None, op0=AluOpType.mult).then_inc(S["sC"], 1)

                @blk.vector
                def _(v, p=p):
                    for stn in range(2):
                        for qb in range(QBn):
                            g = stn * 4 + qb
                            v.wait_ge(S["sQg"], 12 * p + g + 1)
                            v.wait_ge(S["sQe"], 12 * p + g + 1)
                            lo_off = 2048 if stn == 0 else 6144
                            hi_off = 0 if stn == 0 else 4096
                            v.scalar_tensor_tensor(out=qkbf[:, lo_off + qb * 512:lo_off + qb * 512 + 512],
                                                   in0=pb[g % 2][:, 0:512], scalar=QS,
                                                   in1=qkbf[:, hi_off + qb * 512:hi_off + qb * 512 + 512],
                                                   op0=AluOpType.mult, op1=AluOpType.subtract).then_inc(S["sQeV"], 1)
                    for i4 in range(16):
                        v.wait_ge(S["sCb"], 16 * p + i4 + 1)
                        v.reduce_max(out=cmax[:, 4 * i4:4 * i4 + 1], in_=pb[i4 % 2][:, 0:512], axis=AX).then_inc(S["sCr"], 1)
                        if i4 == 11:
                            v.wait_ge(S["sCr"], 16 * p + 12)
                            _combine(v, 0)
                        if i4 == 15:
                            v.wait_ge(S["sCr"], 16 * p + 16)
                            _combine(v, 1)

                    for qb in range(QBn):
                        for h in range(2):
                            gl = qb * 2 + h
                            G = 8 * p + gl
                            v.wait_ge(S["sRe"], G + 1)
                            if G >= 2:
                                v.wait_ge(S["sAH1" if G % 2 else "sAH0"], 16 * (G // 2))
                            v.tensor_tensor(out=attnT[0:64, (G % 2) * 512:(G % 2) * 512 + 512],
                                            in0=pb[4 + (qb + h) % 2][0:64, 0:512],
                                            in1=b2rep[0:64, (gl % 2) * 512:(gl % 2) * 512 + 512],
                                            op=AluOpType.mult).then_inc(S["sAT"], 1)

        for sbk in range(SBn):
            with nc.Block() as blk:
                @blk.sync
                def _(sync, sbk=sbk):
                    if sbk == 0:
                        sync.dma_start(out=b2rep[:], in_=bass.AP(b2_d, 0, [[1024, 128], [1, 1024]])).then_inc(S["sB2r"], 16)
                    for ttoff in range(TTS):
                        gt = sbk * 2 + ttoff
                        if sbk == 0 and ttoff == 0:
                            sync.wait_ge(S["sAH0"], 16 * 32)
                            sync.wait_ge(S["sAH1"], 16 * 32)
                        src = bass.AP(ascr, gt * 128, [[SEQ, 128], [128 * SEQ, 8], [1, 128]])
                        dst = bass.AP(aT, (gt % 2) * 1024, [[2048, 128], [1, 1024]])
                        sync.dma_start(out=dst, in_=src).then_inc(S["sAg1" if gt % 2 else "sAg0"], 16)
                        sync.dma_start(out=A1[:, 8192 + (gt % 2) * 1024:8192 + (gt % 2) * 1024 + 1024],
                                       in_=bass.AP(x_in, gt * 128 * 1024, [[1024, 128], [1, 1024]])).then_inc(S["sX31" if gt % 2 else "sX30"], 16)
                    for fc in range(FC):
                        gfc = sbk * 32 + fc
                        if gfc - 1 > 0:
                            sync.wait_ge(S["sF1"], gfc - 1)
                        src = bass.AP(w1_d, fc * D * 128, [[128, 128], [128 * 128, 8], [1, 128]])
                        dst = bass.AP(A2, 12288 + (fc % 2) * 1024, [[16384, 128], [128, 8], [1, 128]])
                        sync.dma_start(out=dst, in_=src).then_inc(S["sW11" if fc % 2 else "sW10"], 16)
                    for fc in range(FC):
                        gfc = sbk * 32 + fc
                        if gfc - 1 > 0:
                            sync.wait_ge(S["sF2c"], gfc - 1)
                        src = bass.AP(w2_d, fc * 128 * 1024, [[1024, 128], [1, 1024]])
                        dst = bass.AP(A2, 14336 + (fc % 2) * 1024, [[16384, 128], [1, 1024]])
                        sync.dma_start(out=dst, in_=src).then_inc(S["sW21" if fc % 2 else "sW20"], 16)
                    if debug and sbk == 0:
                        sync.wait_ge(S["sRd"], 2)
                        sync.dma_start(out=bass.AP(dbg["d_x1"], 0, [[2048, 128], [1, 2048]]), in_=A2[:, 2048:4096]).then_inc(S["sOut"], 16)
                        sync.wait_ge(S["sTe3"], 4)
                        sync.dma_start(out=bass.AP(dbg["d_ln2T"], 0, [[2048, 128], [1, 2048]]), in_=A2[:, 0:2048]).then_inc(S["sOut"], 16)
                        sync.wait_ge(S["sGe3"], 1)
                        sync.dma_start(out=bass.AP(dbg["d_gT"], 0, [[512, 128], [1, 512]]), in_=A1[:, 0:512]).then_inc(S["sOut"], 16)
                        sync.wait_ge(S["sG"], 1)
                        sync.dma_start(out=bass.AP(dbg["d_rt"], 0, [[48, 128], [1, 8]]), in_=probs[:]).then_inc(S["sOut"], 16)
                        sync.dma_start(out=bass.AP(dbg["d_rt"], 8, [[48, 128], [1, 8]]), in_=nmax[:]).then_inc(S["sOut"], 16)
                        sync.dma_start(out=bass.AP(dbg["d_rt"], 16, [[48, 128], [1, 32]]), in_=rts[:]).then_inc(S["sOut"], 16)
                        sync.wait_ge(S["sG"], 2)
                        sync.dma_start(out=bass.AP(dbg["d_sgate"], 0, [[8, 128], [1, 8]]), in_=s_sb[:]).then_inc(S["sOut"], 16)
                    for ttoff in range(TTS):
                        gt = sbk * 2 + ttoff
                        sync.wait_ge(S["sX2"], gt + 1)
                        sync.dma_start(out=bass.AP(out_d, gt * 128 * 1024, [[1024, 128], [1, 1024]]),
                                       in_=x2b[:]).then_inc(S["sOut"], 16)
                    if sbk == SBn - 1:
                        sync.wait_ge(S["sOut"], 16 * 16 + (11 * 16 if debug else 0))

                @blk.tensor
                def _(tensor, sbk=sbk):
                    for ttoff in range(TTS):
                        gt = sbk * 2 + ttoff
                        tensor.wait_ge(S["sAg1" if gt % 2 else "sAg0"], 16 * (gt // 2 + 1))
                        tensor.wait_ge(S["sWoD"], 16)
                        if ttoff == 1:
                            tensor.wait_ge(S["sRd"], gt)
                        for dob in range(2):
                            for pr in range(8):
                                i = tensor.matmul(pb[dob][:, 0:512],
                                                  aT[:, (gt % 2) * 1024 + pr * 128:(gt % 2) * 1024 + pr * 128 + 128],
                                                  A2[:, 4096 + pr * 1024 + dob * 512:4096 + pr * 1024 + dob * 512 + 512],
                                                  start=(pr == 0), stop=(pr == 7))
                            i.then_inc(S["sY"], 1)
                    for ttoff in range(TTS):
                        gt = sbk * 2 + ttoff
                        tensor.wait_ge(S["sN3"], gt + 1)
                        if ttoff == 1:
                            tensor.wait_ge(S["sTe3"], 2 * gt)
                        for di in range(DI):
                            bank = 6 + di // 4
                            i = tensor.transpose(pb[bank][:, (di % 4) * 128:(di % 4) * 128 + 128],
                                                 nrm[:, di * 128:di * 128 + 128], ident[:])
                            if di == 3 or di == 7:
                                i.then_inc(S["sTr3"], 1)
                        tensor.wait_ge(S["sTe3"], 2 * gt + 2)
                        if gt == 0:
                            tensor.wait_ge(S["sRw"], 16)
                        if gt > 0:
                            tensor.wait_ge(S["sRe3"], gt)
                        for di in range(DI):
                            i = tensor.matmul(pb[2][:, 0:8], A2[:, di * 256 + ttoff * 128:di * 256 + ttoff * 128 + 128],
                                              rws[:, di * 8:di * 8 + 8], start=(di == 0), stop=(di == DI - 1))
                        i.then_inc(S["sRt"], 1)
                    tensor.wait_ge(S["sTe3"], 4 * sbk + 4)
                    for fc in range(FC):
                        gfc = sbk * 32 + fc
                        tensor.wait_ge(S["sW11" if fc % 2 else "sW10"], 16 * (gfc // 2 + 1))
                        if gfc - 1 > 0:
                            tensor.wait_ge(S["sGe3"], gfc - 1)
                        for di in range(DI):
                            i = tensor.matmul(pb[4 + fc % 2][:, 0:256],
                                              A2[:, 12288 + (fc % 2) * 1024 + di * 128:12288 + (fc % 2) * 1024 + di * 128 + 128],
                                              A2[:, di * 256:di * 256 + 256], start=(di == 0), stop=(di == DI - 1))
                        i.then_inc(S["sF1"], 1)
                    tensor.wait_ge(S["sRd"], 2 * sbk + 2)
                    tensor.wait_ge(S["sRe3"], 2 * sbk + 2)
                    for fc in range(FC):
                        gfc = sbk * 32 + fc
                        tensor.wait_ge(S["sW21" if fc % 2 else "sW20"], 16 * (gfc // 2 + 1))
                        tensor.wait_ge(S["sGe3"], gfc + 1)
                        for ttoff in range(TTS):
                            for dob in range(2):
                                i = tensor.matmul(pb[ttoff * 2 + dob][:, 0:512],
                                                  A1[:, fc * 256 + ttoff * 128:fc * 256 + ttoff * 128 + 128],
                                                  A2[:, 14336 + (fc % 2) * 1024 + dob * 512:14336 + (fc % 2) * 1024 + dob * 512 + 512],
                                                  start=(fc == 0), stop=(fc == FC - 1))
                        i.then_inc(S["sF2c"], 1)

                @blk.scalar
                def _(sc, sbk=sbk):
                    for ttoff in range(TTS):
                        gt = sbk * 2 + ttoff
                        sc.wait_ge(S["sRd"], gt + 1)
                        if gt >= 1:
                            sc.wait_ge(S["sSq3"], gt)
                        sc.activation(out=zb[:], in_=A2[:, 2048 + ttoff * 1024:2048 + ttoff * 1024 + 1024],
                                      func=AF.Square, bias=0.0, scale=1.0,
                                      accum_out=st2[:, 8 + 4 * ttoff:8 + 4 * ttoff + 1]).then_inc(S["sSq3"], 1)
                    sc.wait_ge(S["sVa3"], sbk + 1)
                    ssy(sc.activation(out=st2[:, 56:64], in_=st2[:, 48:56], func=AF.Ln, bias=0.0, scale=1.0), sc)
                    sc.activation(out=st2[:, 64:72], in_=st2[:, 56:64], func=AF.Exp, bias=0.0, scale=-0.5).then_inc(S["sSc3"], 1)
                    for ttoff in range(TTS):
                        gt = sbk * 2 + ttoff
                        for half in range(2):
                            sc.wait_ge(S["sTr3"], 2 * gt + half + 1)
                            outap = bass.AP(A2, half * 4 * 256 + ttoff * 128, [[16384, 128], [256, 4], [1, 128]])
                            inap = bass.AP(pb[6 + half], 0, [[512, 128], [128, 4], [1, 128]])
                            sc.activation(out=outap, in_=inap, func=AF.Copy, bias=0.0, scale=1.0).then_inc(S["sTe3"], 1)
                        sc.wait_ge(S["sRm"], gt + 1)
                        if gt > 0:
                            sc.wait_ge(S["sG"], gt)
                        sc.activation(out=probs[:], in_=pb[2][:, 0:8], func=AF.Exp,
                                      bias=nmax[:, 4 * ttoff:4 * ttoff + 1], scale=1.0).then_inc(S["sRe3"], 1)
                    if sbk == 0:
                        sc.wait_ge(S["sB1"], 16)
                    for fc in range(FC):
                        gfc = sbk * 32 + fc
                        sc.wait_ge(S["sF1"], gfc + 1)
                        sc.activation(out=A1[:, fc * 256:fc * 256 + 256], in_=pb[4 + fc % 2][:, 0:256],
                                      func=AF.Gelu_apprx_tanh, bias=b1s[:, 4 * fc:4 * fc + 1],
                                      scale=1.0).then_inc(S["sGe3"], 1)

                @blk.vector
                def _(v, sbk=sbk):
                    for ttoff in range(TTS):
                        gt = sbk * 2 + ttoff
                        v.wait_ge(S["sX31" if gt % 2 else "sX30"], 16 * (gt // 2 + 1))
                        for dob in range(2):
                            v.wait_ge(S["sY"], gt * 2 + dob + 1)
                            vsy(v.tensor_tensor(out=A2[:, 2048 + ttoff * 1024 + dob * 512:2048 + ttoff * 1024 + dob * 512 + 512],
                                                in0=pb[dob][:, 0:512],
                                                in1=A1[:, 8192 + (gt % 2) * 1024 + dob * 512:8192 + (gt % 2) * 1024 + dob * 512 + 512],
                                                op=AluOpType.add), v)
                        v.tensor_reduce(out=st2[:, 4 * ttoff:4 * ttoff + 1],
                                        in_=A2[:, 2048 + ttoff * 1024:2048 + ttoff * 1024 + 1024],
                                        axis=AX, op=AluOpType.add).then_inc(S["sRd"], 1)
                    v.wait_ge(S["sSq3"], 2 * sbk + 2)
                    v.wait_ge(S["sRd"], 2 * sbk + 2)
                    vsy(v.tensor_scalar(out=st2[:, 16:24], in0=st2[:, 0:8], scalar1=1.0 / D, scalar2=None, op0=AluOpType.mult), v)
                    vsy(v.tensor_scalar(out=st2[:, 24:32], in0=st2[:, 8:16], scalar1=1.0 / D, scalar2=None, op0=AluOpType.mult), v)
                    vsy(v.scalar_tensor_tensor(out=st2[:, 32:40], in0=st2[:, 16:24], scalar=-1.0,
                                               in1=st2[:, 16:24], op0=AluOpType.mult, op1=AluOpType.mult), v)
                    vsy(v.tensor_tensor(out=st2[:, 40:48], in0=st2[:, 24:32], in1=st2[:, 32:40], op=AluOpType.add), v)
                    v.tensor_scalar(out=st2[:, 48:56], in0=st2[:, 40:48], scalar1=1e-5, scalar2=None, op0=AluOpType.add).then_inc(S["sVa3"], 1)
                    v.wait_ge(S["sSc3"], sbk + 1)
                    for ttoff in range(TTS):
                        gt = sbk * 2 + ttoff
                        if ttoff == 1:
                            v.wait_ge(S["sTr3"], 2 * gt)
                        v.tensor_scalar(out=nrm[:], in0=A2[:, 2048 + ttoff * 1024:2048 + ttoff * 1024 + 1024],
                                        scalar1=st2[:, 16 + 4 * ttoff:16 + 4 * ttoff + 1],
                                        scalar2=st2[:, 64 + 4 * ttoff:64 + 4 * ttoff + 1],
                                        op0=AluOpType.subtract, op1=AluOpType.mult).then_inc(S["sN3"], 1)
                        v.wait_ge(S["sRt"], gt + 1)
                        v.reduce_max(out=nmax[:, 4 * ttoff:4 * ttoff + 1], in_=pb[2][:, 0:8], axis=AX, negate=True).then_inc(S["sRm"], 1)
                        v.wait_ge(S["sRe3"], gt + 1)
                        vsy(v.reduce_max(out=rts[:, 0:1], in_=probs[:], axis=AX), v)
                        vsy(v.tensor_scalar(out=lts[:], in0=probs[:], scalar1=rts[:, 0:1], scalar2=None, op0=AluOpType.is_lt), v)
                        vsy(v.tensor_tensor(out=msk[:], in0=probs[:], in1=lts[:], op=AluOpType.mult), v)
                        vsy(v.reduce_max(out=rts[:, 4:5], in_=msk[:], axis=AX), v)
                        vsy(v.tensor_tensor(out=rts[:, 8:9], in0=rts[:, 0:1], in1=rts[:, 4:5], op=AluOpType.add), v)
                        vsy(v.reciprocal(out=rts[:, 12:13], in_=rts[:, 8:9]), v)
                        vsy(v.tensor_tensor(out=rts[:, 16:17], in0=rts[:, 0:1], in1=rts[:, 12:13], op=AluOpType.mult), v)
                        vsy(v.tensor_tensor(out=rts[:, 20:21], in0=rts[:, 4:5], in1=rts[:, 12:13], op=AluOpType.mult), v)
                        v.tensor_tensor(out=s_sb[:, 4 * ttoff:4 * ttoff + 1], in0=rts[:, 16:17], in1=rts[:, 20:21],
                                        op=AluOpType.add).then_inc(S["sG"], 1)
                    if sbk == 0:
                        v.wait_ge(S["sB2r"], 16)
                    v.wait_ge(S["sF2c"], 32 * (sbk + 1))
                    v.wait_ge(S["sG"], 2 * sbk + 2)
                    for ttoff in range(TTS):
                        gt = sbk * 2 + ttoff
                        if gt > 0:
                            v.wait_ge(S["sOut"], 16 * gt)
                        for dob in range(2):
                            vsy(v.tensor_tensor(out=zb[:, dob * 512:dob * 512 + 512], in0=pb[ttoff * 2 + dob][:, 0:512],
                                                in1=b2rep[:, dob * 512:dob * 512 + 512], op=AluOpType.add), v)
                        v.scalar_tensor_tensor(out=x2b[:], in0=zb[:], scalar=s_sb[:, 4 * ttoff:4 * ttoff + 1],
                                               in1=A2[:, 2048 + ttoff * 1024:2048 + ttoff * 1024 + 1024],
                                               op0=AluOpType.mult, op1=AluOpType.add).then_inc(S["sX2"], 1)

    return nc


_NC_CACHE = {}


def kernel(L0_attn_x, L0_attn_gamma, L0_attn_w_qkv, L0_attn_w_out,
           L0_ffn_router_w, L0_ffn_w1, L0_ffn_b1, L0_ffn_w2, L0_ffn_b2):
    x = np.ascontiguousarray(np.asarray(L0_attn_x, dtype=np.float32))
    gam = np.ascontiguousarray(np.asarray(L0_attn_gamma, dtype=np.float32))
    wqkv = np.ascontiguousarray(np.asarray(L0_attn_w_qkv, dtype=np.float32))
    wout = np.ascontiguousarray(np.asarray(L0_attn_w_out, dtype=np.float32))
    rw = np.ascontiguousarray(np.asarray(L0_ffn_router_w, dtype=np.float32))
    w1 = np.ascontiguousarray(np.asarray(L0_ffn_w1, dtype=np.float32))
    b1 = np.ascontiguousarray(np.asarray(L0_ffn_b1, dtype=np.float32))
    w2 = np.ascontiguousarray(np.asarray(L0_ffn_w2, dtype=np.float32))
    b2 = np.ascontiguousarray(np.asarray(L0_ffn_b2, dtype=np.float32))

    b1l = np.zeros((128, 128), dtype=np.float32)
    for fc in range(32):
        b1l[:, 4 * fc] = b1[fc * 128:(fc + 1) * 128]
    b2rep = np.ascontiguousarray(np.broadcast_to(b2[None, :], (128, D)))
    rwl = np.ascontiguousarray(rw.reshape(8, 128, 8).transpose(1, 0, 2).reshape(128, 64))
    ident = np.eye(128, dtype=np.float32)

    if "nc" not in _NC_CACHE:
        _NC_CACHE["nc"] = build_nc()
    nc = _NC_CACHE["nc"]

    wqkvp = np.ascontiguousarray(wqkv.reshape(D, 3, 8, 128).transpose(2, 0, 1, 3).reshape(8, D, 384))
    w1p = np.ascontiguousarray(w1.reshape(D, 32, 128).transpose(1, 0, 2))
    common = {"wqkv": wqkvp, "wout": wout, "w1": w1p, "w2": w2, "gam": gam,
              "b1l": b1l, "b2rep": b2rep, "rwl": rwl, "ident": ident}
    in_maps = [dict(common, x_in=np.ascontiguousarray(x[b])) for b in range(8)]
    res = run_bass_kernel_spmd(nc, in_maps, core_ids=list(range(8)))
    return np.stack([np.asarray(res.results[i]["out_x"]) for i in range(8)], axis=0)


if __name__ == "__main__":
    rng = np.random.default_rng(0)
    ins = {
        "L0_attn_x": rng.standard_normal((8, SEQ, D)).astype(np.float32),
        "L0_attn_gamma": np.zeros(D, np.float32),
        "L0_attn_w_qkv": (rng.standard_normal((D, 3 * D)) * D ** -0.5).astype(np.float32),
        "L0_attn_w_out": (rng.standard_normal((D, D)) * D ** -0.5).astype(np.float32),
        "L0_ffn_router_w": (rng.standard_normal((D, 8)) * 0.02).astype(np.float32),
        "L0_ffn_w1": (rng.standard_normal((D, 4 * D)) * 0.02).astype(np.float32),
        "L0_ffn_b1": np.zeros(4 * D, np.float32),
        "L0_ffn_w2": (rng.standard_normal((4 * D, D)) * 0.02).astype(np.float32),
        "L0_ffn_b2": np.zeros(D, np.float32),
    }
    out = kernel(**ins)
    print("out", out.shape, out.dtype, np.abs(out).mean())
